# version 03
# baseline (speedup 1.0000x reference)
import math
import bisect
import numpy as np

import concourse.bass as bass
import concourse.mybir as mybir
from concourse.bass_utils import run_bass_kernel_spmd

F32 = mybir.dt.float32
I32 = mybir.dt.int32
BF16 = mybir.dt.bfloat16
AF = mybir.ActivationFunctionType
ALU = mybir.AluOpType

B_TOT, N_CORES = 16, 8
B_LOC = B_TOT // N_CORES
S, E, H, HD, KK = 2048, 1024, 16, 64, 4
F_FFN, NE, R_LORA = 4096, 8, 16
D_INNER, D_STATE, D_CONV = 2048, 64, 4
MAX_SEQ, ROPE_BASE = 8192, 10000.0
EC = E // 128
FC = F_FFN // 128
CC = D_INNER // 128
SHALF = S // 2

ENGINES = ("sp", "pe", "act", "dve", "pool")


class Prog:

    NLANES = 8

    def __init__(self):
        self.ops = []
        self.ccount = {e: 0 for e in ENGINES}
        self.dcount = {e: 0 for e in ENGINES}
        self.lcount = {}

    def comp(self, eng, fn, deps=()):
        import sys
        f = sys._getframe(2)
        tok = ("c", eng, self.ccount[eng] + 1)
        self.ccount[eng] += 1
        self.ops.append((eng, "c", fn, tuple(deps), tok))
        self.labels = getattr(self, "labels", {})
        self.labels[tok] = f"{f.f_code.co_name}:{f.f_lineno}"
        return tok

    def dma(self, eng, fn, deps=()):
        import sys
        f = sys._getframe(2)
        lane = (eng, self.dcount[eng] % self.NLANES)
        self.dcount[eng] += 1
        k = self.lcount.get(lane, 0) + 1
        self.lcount[lane] = k
        tok = ("d", lane, 16 * k)
        self.ops.append((eng, "d", fn, tuple(deps), tok))
        self.labels = getattr(self, "labels", {})
        self.labels[tok] = f"{f.f_code.co_name}:{f.f_lineno}"
        return tok

    def barrier(self):
        toks = []
        for e in ENGINES:
            if self.ccount[e]:
                toks.append(("c", e, self.ccount[e]))
        for lane, k in self.lcount.items():
            toks.append(("d", lane, 16 * k))
        return toks

    def build_plans(self, sim_edges=False):
        if sim_edges:
            ops = []
            prev = {e: None for e in ENGINES}
            for eng, kind, fn, deps, tok in self.ops:
                if kind == "c":
                    d = deps + ((prev[eng],) if prev[eng] else ())
                    prev[eng] = tok
                    ops.append((eng, kind, fn, d, tok))
                else:
                    ops.append((eng, kind, fn, deps, tok))
        else:
            ops = []
            for eng, kind, fn, deps, tok in self.ops:
                if kind == "c":
                    deps = tuple(t for t in deps
                                 if not (t[0] == "c" and t[1] == eng))
                ops.append((eng, kind, fn, deps, tok))
        needed = {e: set() for e in ENGINES}
        for op in ops:
            for t in op[3]:
                if t[0] == "c":
                    needed[t[1]].add(t[2])
        nsort = {e: sorted(s) for e, s in needed.items()}
        know = {e: {} for e in ENGINES}
        plans = {e: [] for e in ENGINES}
        for eng, kind, fn, deps, tok in ops:
            k = know[eng]
            waits = []
            if kind == "d":
                key = ("d", tok[1])
                pv = tok[2] - 16
                if pv > 0 and k.get(key, 0) < pv:
                    waits.append((key, pv))
                    k[key] = pv
            for t in deps:
                key = (t[0], t[1])
                if t[0] == "c":
                    if not nsort[t[1]]:
                        continue
                    v = bisect.bisect_right(nsort[t[1]], t[2])
                    if v == 0:
                        continue
                else:
                    v = t[2]
                if k.get(key, 0) < v:
                    waits.append((key, v))
                    k[key] = v
            if kind == "c":
                inc = (("c", eng), 1) if tok[2] in needed[eng] else None
            else:
                inc = (("d", tok[1]), 16)
            plans[eng].append((waits, fn, inc))
        semkeys = set()
        for e in ENGINES:
            if nsort[e]:
                semkeys.add(("c", e))
        for lane in self.lcount:
            semkeys.add(("d", lane))
        return plans, semkeys



def _cody_consts():
    c = 2.0 * math.pi
    c1 = float(np.float32(np.ldexp(np.round(np.ldexp(c, 11)), -11)))
    r = c - c1
    c2 = float(np.float32(np.ldexp(np.round(np.ldexp(r, 26)), -26)))
    r -= c2
    c3 = float(np.float32(r))
    return c1, c2, c3


C1_2PI, C2_2PI, C3_2PI = _cody_consts()
INV_2PI = float(np.float32(1.0 / (2.0 * math.pi)))
HALF_PI = float(np.float32(math.pi / 2.0))
LN_EPS = 1e-5


def host_constants():
    cst = {}
    cst["ident"] = np.eye(128, dtype=np.float32)
    cst["onesq"] = np.ones((128, 128), dtype=np.float32)
    cst["trilx"] = np.triu(np.ones((128, 128), dtype=np.float32), 1)
    m = np.zeros((128, 128), dtype=np.float32)
    m[np.triu_indices(128, 1)] = -1e30
    cst["cmask"] = m
    e16 = np.zeros((8, 128), dtype=np.float32)
    for e in range(8):
        e16[e, e * 16:(e + 1) * 16] = 1.0
    cst["e16x1"] = e16
    cst["e16x2"] = (2.0 * e16).astype(np.float32)
    inv_freq = (1.0 / ROPE_BASE ** (np.arange(0, HD, 2, dtype=np.float32) / HD)
                ).astype(np.float32)
    fk = np.tile(inv_freq, KK)
    cst["f_col"] = fk.reshape(128, 1)
    cst["cm1_col"] = (np.cos(fk) - 1.0).astype(np.float32).reshape(128, 1)
    cst["sf_col"] = np.sin(fk).astype(np.float32).reshape(128, 1)
    kd = np.zeros((128, 64), dtype=np.float32)
    for k in range(KK):
        for d in range(32):
            kd[k * 32 + d, d] = 1.0
            kd[k * 32 + d, d + 32] = 1.0
    cst["kdsum"] = kd
    sg = np.ones((128, 1), dtype=np.float32)
    for hh in range(2):
        sg[hh * 64:hh * 64 + 32, 0] = -1.0
    cst["rotsgn"] = sg
    rp = np.zeros((128, 128), dtype=np.float32)
    for base in (0, 32, 64):
        for k in range(KK):
            rp[base + k, k * 32:(k + 1) * 32] = 1.0
    cst["rep4"] = rp
    return cst


def host_layout(inp):
    d = {}
    a = lambda v: np.ascontiguousarray(np.asarray(v, dtype=np.float32))
    d["wq"] = a(inp["L0_attn_Wq"])
    d["wk"] = a(inp["L0_attn_Wk"])
    d["wv"] = a(inp["L0_attn_Wv"])
    d["wr"] = a(inp["L0_attn_Wr"])
    d["wo"] = a(inp["L0_attn_Wo"])
    d["gwT"] = a(np.asarray(inp["L0_ffn_gate_w"]).T)
    d["wg1T"] = a(np.asarray(inp["L0_ffn_w_gate"]).T)
    d["wu1T"] = a(np.asarray(inp["L0_ffn_w_up"]).T)
    d["wd1T"] = a(np.asarray(inp["L0_ffn_w_down"]).T)
    d["a1s"] = a(np.asarray(inp["L0_ffn_A1"]).transpose(2, 0, 1).reshape(1024, 128))
    d["a3s"] = a(np.asarray(inp["L0_ffn_A3"]).transpose(2, 0, 1).reshape(1024, 128))
    d["b1s"] = a(np.asarray(inp["L0_ffn_B1"]).transpose(0, 2, 1).reshape(128, 4096))
    d["b3s"] = a(np.asarray(inp["L0_ffn_B3"]).transpose(0, 2, 1).reshape(128, 4096))
    d["a2s"] = a(np.asarray(inp["L0_ffn_A2"]).transpose(2, 0, 1).reshape(4096, 128))
    d["b2s"] = a(np.asarray(inp["L0_ffn_B2"]).transpose(0, 2, 1).reshape(128, 1024))
    d["nw"] = a(np.asarray(inp["L1_attn_norm_w"]).reshape(8, 128))
    d["nb"] = a(np.asarray(inp["L1_attn_norm_b"]).reshape(8, 128))
    d["win"] = a(inp["L1_attn_Win"])
    d["convw"] = a(np.asarray(inp["L1_attn_conv_w"]).reshape(2048, 4))
    d["convb"] = a(np.asarray(inp["L1_attn_conv_b"]).reshape(16, 128))
    d["dtw"] = a(inp["L1_attn_dt_w"])
    d["dtb"] = a(np.asarray(inp["L1_attn_dt_b"]).reshape(16, 128))
    d["alog"] = a(inp["L1_attn_A_log"])
    d["dvec"] = a(np.asarray(inp["L1_attn_D"]).reshape(16, 128))
    d["bp"] = a(inp["L1_attn_Bp"])
    d["cp"] = a(inp["L1_attn_Cp"])
    d["wout"] = a(inp["L1_attn_Wout"])
    d["wg2T"] = a(np.asarray(inp["L1_ffn_w_gate"]).T)
    d["wu2T"] = a(np.asarray(inp["L1_ffn_w_up"]).T)
    d["wd2T"] = a(np.asarray(inp["L1_ffn_w_down"]).T)
    return d


WEIGHT_SHAPES = {
    "wq": (1024, 1024), "wk": (1024, 1024), "wv": (1024, 1024),
    "wr": (1024, 64), "wo": (1024, 1024), "gwT": (1024, 8),
    "wg1T": (1024, 4096), "wu1T": (1024, 4096), "wd1T": (4096, 1024),
    "a1s": (1024, 128), "a3s": (1024, 128), "b1s": (128, 4096),
    "b3s": (128, 4096), "a2s": (4096, 128), "b2s": (128, 1024),
    "nw": (8, 128), "nb": (8, 128), "win": (1024, 4096),
    "convw": (2048, 4), "convb": (16, 128), "dtw": (2048, 2048),
    "dtb": (16, 128), "alog": (2048, 64), "dvec": (16, 128),
    "bp": (2048, 64), "cp": (2048, 64), "wout": (2048, 1024),
    "wg2T": (1024, 4096), "wu2T": (1024, 4096), "wd2T": (4096, 1024),
}
CONST_SHAPES = {
    "ident": (128, 128), "onesq": (128, 128), "trilx": (128, 128),
    "cmask": (128, 128), "e16x1": (8, 128), "e16x2": (8, 128),
    "f_col": (128, 1), "cm1_col": (128, 1), "sf_col": (128, 1),
    "kdsum": (128, 64), "rotsgn": (128, 1), "rep4": (128, 128),
}

DEBUG = []
STAGE = "all"
NODE_TCS = 16
NODE_PHASE = 4


class Ctx:
    pass


class Arena:
    def __init__(self, ap, elems, name="ar"):
        self.ap = ap
        self.elems = elems
        self.off = 0
        self.marks = []
        self.name = name

    def alloc(self, n):
        assert self.off + n <= self.elems, \
            f"{self.name} overflow {self.off}+{n}>{self.elems}"
        ap = self.ap[:, self.off:self.off + n]
        self.off += n
        return ap

    def push(self):
        self.marks.append(self.off)

    def pop(self):
        self.off = self.marks.pop()



def build_nc():
    nc = bass.Bass("TRN2", target_bir_lowering=False)
    C = Ctx()
    C.nc = nc
    C.P = Prog()
    C.x = nc.dram_tensor("x", [B_LOC, E, S], F32, kind="ExternalInput").ap()
    C.out = nc.dram_tensor("out", [B_LOC, E, S], F32, kind="ExternalOutput").ap()
    C.w = {}
    for n, sh in WEIGHT_SHAPES.items():
        C.w[n] = nc.dram_tensor(n, list(sh), F32, kind="ExternalInput").ap()
    for n, sh in CONST_SHAPES.items():
        C.w[n] = nc.dram_tensor(n, list(sh), F32, kind="ExternalInput").ap()
    C.x1scr = nc.dram_tensor("x1scr", [B_LOC, E, S], F32, kind="Internal").ap()
    C.bcscr = nc.dram_tensor("bcscr", [128, 512], F32, kind="Internal").ap()
    C.dbg = {}
    for name, sh in DEBUG:
        C.dbg[name] = nc.dram_tensor("dbg_" + name, list(sh), F32,
                                     kind="ExternalOutput").ap()
    return C


def finish(C, stack):
    import os as _os
    nc, P = C.nc, C.P
    plans, semkeys = P.build_plans(sim_edges=True)
    sems = {}
    for k in sorted(semkeys, key=str):
        nm = "sem_" + "_".join(str(p) for p in (k[1] if isinstance(k[1], tuple) else (k[1],))) + "_" + k[0]
        sems[k] = stack.enter_context(nc.semaphore(nm))
    with nc.Block() as block:
        def mk(e):
            def body(engh):
                for waits, fn, inc in plans[e]:
                    for (tk, v) in waits:
                        engh.wait_ge(sems[tk], v)
                    instr = fn(engh)
                    if inc is not None and instr is not None:
                        instr.then_inc(sems[inc[0]], inc[1])
            return body
        names = {"sp": "sync", "pe": "tensor", "act": "scalar",
                 "dve": "vector", "pool": "gpsimd"}
        for e in ENGINES:
            if plans[e]:
                getattr(block, names[e])(mk(e))



def e_dma(C, q, out_ap, in_ap, deps=()):
    return C.P.dma(q, lambda e, o=out_ap, i=in_ap: e.dma_start(out=o, in_=i), deps)


def e_mm(C, out_ap, lhsT, rhs, start, stop, deps=(), skipg=False):
    return C.P.comp("pe", lambda e, o=out_ap, l=lhsT, r=rhs, st=start, sp=stop,
                    sk=skipg: e.matmul(o, l, r, start=st, stop=sp,
                                       skip_group_check=sk), deps)


def e_trans(C, out_ap, in_ap, deps=()):
    p = in_ap.shape[0]
    bp = in_ap.base_partition()
    return C.P.comp("pe", lambda e, o=out_ap, i=in_ap, n=p, b=bp:
                    e.transpose(o, i, C.c_ident[b:b + n, b:b + n]), deps)


def e_act(C, out_ap, in_ap, func, bias=0.0, scale=1.0, accum=None, deps=()):
    return C.P.comp("act", lambda e, o=out_ap, i=in_ap, f=func, b=bias, s=scale,
                    a=accum: e.activation(out=o, in_=i, func=f, bias=b, scale=s,
                                          accum_out=a), deps)


def e_copy_act(C, out_ap, in_ap, deps=()):
    return C.P.comp("act", lambda e, o=out_ap, i=in_ap: e.copy(out=o, in_=i), deps)


def e_tt(C, out_ap, in0, in1, op, deps=()):
    return C.P.comp("dve", lambda e, o=out_ap, a=in0, b=in1, p=op:
                    e.tensor_tensor(out=o, in0=a, in1=b, op=p), deps)


def e_ts(C, out_ap, in0, s1, op0, s2=None, op1=None, deps=(), eng="dve"):
    if op1 is None:
        return C.P.comp(eng, lambda e, o=out_ap, i=in0, a=s1, p=op0:
                        e.tensor_scalar(out=o, in0=i, scalar1=a, scalar2=None,
                                        op0=p), deps)
    return C.P.comp(eng, lambda e, o=out_ap, i=in0, a=s1, p=op0, b=s2, q=op1:
                    e.tensor_scalar(out=o, in0=i, scalar1=a, scalar2=b,
                                    op0=p, op1=q), deps)


def e_stt(C, out_ap, in0, scalar, in1, op0, op1, deps=()):
    return C.P.comp("dve", lambda e, o=out_ap, i=in0, s=scalar, j=in1, p=op0,
                    q=op1: e.scalar_tensor_tensor(out=o, in0=i, scalar=s, in1=j,
                                                  op0=p, op1=q), deps)


def e_red(C, out_ap, in_ap, op, axis="X", absolute=False, negate=False, deps=(),
          eng="dve"):
    ax = getattr(mybir.AxisListType, axis)
    return C.P.comp(eng, lambda e, o=out_ap, i=in_ap, p=op, x=ax, ab=absolute,
                    n=negate: e.tensor_reduce(out=o, in_=i, axis=x, op=p,
                                              apply_absolute_value=ab,
                                              negate=n), deps)


def e_scan(C, out_ap, d0, d1, initial, deps=()):
    return C.P.comp("dve", lambda e, o=out_ap, a=d0, b=d1, i=initial:
                    e.tensor_tensor_scan(out=o, data0=a, data1=b, initial=i,
                                         op0=ALU.mult, op1=ALU.add), deps)


def e_recip(C, out_ap, in_ap, deps=()):
    return C.P.comp("dve", lambda e, o=out_ap, i=in_ap:
                    e.reciprocal(out=o, in_=i), deps)


def e_copy(C, out_ap, in_ap, deps=(), eng="dve"):
    return C.P.comp(eng, lambda e, o=out_ap, i=in_ap:
                    e.tensor_copy(out=o, in_=i), deps)


def e_memset(C, ap, val, deps=(), eng="dve"):
    return C.P.comp(eng, lambda e, o=ap, v=val: e.memset(o, v), deps)


def e_floor(C, out_ap, in_ap, iscr, scr, deps=()):
    t1 = e_copy(C, iscr, in_ap, deps=deps)
    t2 = e_copy(C, out_ap, iscr, deps=[t1])
    t3 = e_tt(C, scr, out_ap, in_ap, ALU.is_gt, deps=[t2] + list(deps))
    t4 = e_tt(C, out_ap, out_ap, scr, ALU.subtract, deps=[t3])
    return t4



def build_program(C):
    nc, P = C.nc, C.P
    import contextlib
    stack = contextlib.ExitStack()
    with stack:
        AR_ELEMS = 51500
        C.ar = Arena(stack.enter_context(
            nc.sbuf_tensor("AR", [128, AR_ELEMS], F32)), AR_ELEMS, "AR")
        C.iscr = stack.enter_context(nc.sbuf_tensor("ISCR", [128, 1024], I32))
        C.psA = stack.enter_context(nc.psum_tensor("psA", [128, 2048], F32))
        C.psB = stack.enter_context(nc.psum_tensor("psB", [128, 2048], F32))

        emit_all(C)
        finish(C, stack)
    return C


def bank(C, i):
    return (C.psA if i < 4 else C.psB)[:, (i % 4) * 512:(i % 4 + 1) * 512]


def emit_all(C):
    ar = C.ar
    ct = []
    for nm in ("ident", "ones", "trilx", "cmask", "e16x1", "e16x2",
               "f_col", "cm1_col", "sf_col", "kdsum", "rotsgn", "rep4"):
        src = {"ones": "onesq"}.get(nm, nm)
        sh = CONST_SHAPES[src]
        a = ar.alloc(sh[1])
        setattr(C, "c_" + nm, a)
        ct.append(e_dma(C, "sp", a[0:sh[0], :], C.w[src]))
    ct.append(e_dma(C, "sp", C.c_e16x1[32:40, :], C.w["e16x1"]))
    ct.append(e_dma(C, "sp", C.c_e16x2[32:40, :], C.w["e16x2"]))
    C.tok_consts = ct

    C.c_nw = ar.alloc(8)
    C.c_nb = ar.alloc(8)
    C.c_convb = ar.alloc(16)
    C.c_dtb = ar.alloc(16)
    C.c_dvec = ar.alloc(16)
    C.c_convw = ar.alloc(64)
    vt = []
    for g in range(8):
        vt.append(e_dma(C, "sp", C.c_nw[:, g:g + 1], C.w["nw"][g, :].unsqueeze(-1)))
        vt.append(e_dma(C, "sp", C.c_nb[:, g:g + 1], C.w["nb"][g, :].unsqueeze(-1)))
    for g in range(16):
        vt.append(e_dma(C, "sp", C.c_convb[:, g:g + 1], C.w["convb"][g, :].unsqueeze(-1)))
        vt.append(e_dma(C, "sp", C.c_dtb[:, g:g + 1], C.w["dtb"][g, :].unsqueeze(-1)))
        vt.append(e_dma(C, "sp", C.c_dvec[:, g:g + 1], C.w["dvec"][g, :].unsqueeze(-1)))
        vt.append(e_dma(C, "sp", C.c_convw[:, g * 4:(g + 1) * 4],
                        C.w["convw"][g * 128:(g + 1) * 128, :]))
    C.tok_small = vt

    C.wbuf = ar.alloc(2048)
    C.wbuf_war = []

    C.c_eps8 = ar.alloc(1)
    C.tok_consts.append(e_memset(C, C.c_eps8, 1e-8))
    C.c_one1 = ar.alloc(1)
    C.tok_consts.append(e_memset(C, C.c_one1, 1.0))

    C.halo = ar.alloc(48)
    C.carry = ar.alloc(1024)

    C.c_aneg = ar.alloc(1024)
    at = []
    for g in range(16):
        sl = C.c_aneg[:, g * 64:(g + 1) * 64]
        t0 = e_dma(C, "sp", sl, C.w["alog"][g * 128:(g + 1) * 128, :])
        t1 = e_act(C, sl, sl, AF.Exp, deps=[t0])
        t2 = e_ts(C, sl, sl, -1.0, ALU.mult, deps=[t1])
        at.append(t2)
    C.tok_aneg = at

    for b in range(B_LOC):
        emit_seq(C, b)

    bar = C.P.barrier()
    C.P.comp("sp", lambda e: e.nop(), deps=bar)



def emit_seq(C, b):
    ar = C.ar
    P = C.P
    bar0 = P.barrier()
    ar.push()
    xn = [ar.alloc(S) for _ in range(EC)]
    C.s_xn = xn
    tok_xn = emit_ln_plain(C, C.x[b], xn, war=bar0,
                           dbg=("xn0" if b == 0 else None))
    C.tok_xn = tok_xn
    C.war_attncore = []
    C.war_bank45 = []
    C.war_ropepb = []

    if STAGE != "ln":
        emit_node_prep(C, b)

    ao = [ar.alloc(SHALF) for _ in range(EC)]
    C.s_ao = ao
    if STAGE in ("rope", "attn", "all"):
        nhp = 8 if STAGE != "rope" else 1
        for half in range(2 if STAGE != "rope" else 1):
            tok_ao = emit_attn_half(C, b, half, nhp)
            if "ao0" in C.dbg and b == 0 and half == 0 and STAGE != "rope":
                for g in range(EC):
                    e_dma(C, "pool", C.dbg["ao0"][g * 128:(g + 1) * 128, :],
                          ao[g], deps=tok_ao)
            if STAGE == "all":
                emit_ffn_half(C, b, half)
    if STAGE != "all":
        for g in range(EC):
            for tt in range(4):
                e_dma(C, "pool", C.out[b][g * 128:(g + 1) * 128,
                                          tt * 512:(tt + 1) * 512],
                      xn[g][:, tt * 512:(tt + 1) * 512], deps=tok_xn)
    ar.pop()
    if STAGE == "all":
        emit_l1(C, b)


def emit_ffn_half(C, b, half):
    ar, P = C.ar, C.P
    xn, tok_xn = C.s_xn, C.tok_xn
    ao = C.s_ao
    h0c = half * 1024
    ar.push()
    selA = ar.alloc(SHALF)
    wB = ar.alloc(SHALF)
    r1 = ar.alloc(SHALF)
    r3 = ar.alloc(SHALF)
    rm = ar.alloc(4 * 256)
    mixed = ar.alloc(FC * 256)
    scr = ar.alloc(8 * 256)
    X1ST, C1S, C3S, TA, TB, ACT1, ACT2, TC_ = [scr[:, i * 256:(i + 1) * 256]
                                               for i in range(8)]
    wt = ar.alloc(512)
    war0 = getattr(C, "war_ffn", []) + C.P.barrier()

    def wslot(i, n=1):
        if i >= 16:
            return wt[:, (i - 16) * 128:(i - 16 + n) * 128]
        return C.wbuf[:, i * 128:(i + n) * 128]

    gfm = scr[:, 0:1024]
    tz = e_memset(C, gfm, 0.0, deps=war0)
    tgw = [e_dma(C, "sp", wt[:, 384 + g * 8:384 + (g + 1) * 8],
                 C.w["gwT"][g * 128:(g + 1) * 128, :],
                 deps=war0 + getattr(C, "wbuf_war", []))
           for g in range(EC)]
    ev = None
    for tt in range(2):
        mm = None
        for g in range(EC):
            mm = e_mm(C, bank(C, 0)[0:8, :], wt[:, 384 + g * 8:384 + (g + 1) * 8],
                      xn[g][:, h0c + tt * 512:h0c + (tt + 1) * 512],
                      g == 0, g == EC - 1,
                      deps=[tgw[g]] + tok_xn + ([ev] if ev else list(war0)))
        ev = e_copy_act(C, gfm[0:8, tt * 512:(tt + 1) * 512], bank(C, 0)[0:8, :],
                        deps=[mm, tz])
    tokg = []
    for tt2 in range(8):
        c0 = tt2 * 128
        ttr = e_trans(C, bank(C, 1)[0:128, 0:128], gfm[:, c0:c0 + 128],
                      deps=[ev, tz] + tokg[-4:])
        lg8 = TB[:, 0:8]
        p8 = TB[:, 8:16]
        m8 = TB[:, 16:24]
        colz = TB[:, 24:32]
        s1 = TB[:, 32:40]
        s2 = TB[:, 40:48]
        t0 = e_copy(C, lg8, bank(C, 1)[0:128, 0:8], deps=[ttr])
        t1 = e_red(C, colz[:, 0:1], lg8, ALU.max, deps=[t0])
        t2 = e_ts(C, lg8, lg8, colz[:, 0:1], ALU.subtract, deps=[t1])
        t3 = e_act(C, lg8, lg8, AF.Exp, deps=[t2])
        t4 = e_red(C, colz[:, 1:2], lg8, ALU.add, deps=[t3])
        t5 = e_recip(C, colz[:, 2:3], colz[:, 1:2], deps=[t4])
        t6 = e_ts(C, p8, lg8, colz[:, 2:3], ALU.mult, deps=[t5, t3])
        t7 = P.comp("dve", lambda e, o=m8, i=p8: e.max(out=o, in_=i), deps=[t6])
        t8 = e_tt(C, colz[:, 3:4], m8[:, 0:1], m8[:, 1:2], ALU.add, deps=[t7])
        t9 = e_recip(C, colz[:, 4:5], colz[:, 3:4], deps=[t8])
        ta = e_tt(C, colz[:, 5:6], m8[:, 0:1], colz[:, 4:5], ALU.mult, deps=[t9])
        tb = e_tt(C, colz[:, 6:7], m8[:, 1:2], colz[:, 4:5], ALU.mult, deps=[ta])
        tc1 = e_ts(C, s1, p8, m8[:, 0:1], ALU.is_ge, deps=[t7, t6])
        tc2 = e_ts(C, s2, p8, m8[:, 1:2], ALU.is_ge, deps=[tc1])
        tc3 = e_tt(C, s2, s2, s1, ALU.subtract, deps=[tc2])
        u1 = e_trans(C, bank(C, 2)[0:8, 0:128], s1, deps=[tc3] + tokg[-4:])
        u2 = e_trans(C, bank(C, 2)[0:8, 128:256], s2, deps=[tc3])
        u3 = e_trans(C, bank(C, 2)[0:1, 256:384], colz[:, 5:6], deps=[tb, tc3])
        u3b = e_trans(C, bank(C, 2)[0:1, 384:512], colz[:, 6:7],
                      deps=[tb, tc3])
        g1 = e_copy_act(C, selA[0:8, c0:c0 + 128], bank(C, 2)[0:8, 0:128],
                        deps=[u3, u3b])
        g2 = e_copy_act(C, selA[32:40, c0:c0 + 128], bank(C, 2)[0:8, 128:256],
                        deps=[u3, u3b])
        g3 = e_copy_act(C, wB[0:1, c0:c0 + 128], bank(C, 2)[0:1, 256:384],
                        deps=[u3, u3b])
        g4 = e_copy_act(C, wB[32:33, c0:c0 + 128], bank(C, 2)[0:1, 384:512],
                        deps=[u3, u3b])
        tokg += [g1, g2, g3, g4]
    evr = []
    for nm, dst in (("a1s", r1), ("a3s", r3)):
        tw = [e_dma(C, "sp", wslot(i),
                    C.w[nm][i * 128:(i + 1) * 128, :], deps=war0 + evr)
              for i in range(EC)]
        for tt in range(2):
            mm = None
            for g in range(EC):
                mm = e_mm(C, bank(C, 3), wslot(g),
                          xn[g][:, h0c + tt * 512:h0c + (tt + 1) * 512],
                          g == 0, g == EC - 1,
                          deps=[tw[g]] + tok_xn + evr)
            evr = [e_copy_act(C, dst[:, tt * 512:(tt + 1) * 512], bank(C, 3),
                              deps=[mm])]
    tok_r = evr
    C._ffn_dbg = []
    if b == 0 and half == 0:
        for nm, buf, dps in (("selA0", selA, tokg), ("wB0", wB, tokg),
                             ("r1d", r1, tok_r), ("r3d", r3, tok_r)):
            if nm in C.dbg:
                C._ffn_dbg.append(e_dma(C, "pool", C.dbg[nm], buf, deps=dps))

    prev_dma = list(war0) + tok_r
    prev_b03 = []
    prev_b47 = []
    for tl in range(4):
        t0c = h0c + tl * 256
        tcol = (tl * 256, (tl + 1) * 256)
        mm_s2a = e_mm(C, bank(C, 4)[:, 0:256], C.c_e16x2[0:8, :],
                      selA[0:8, tcol[0]:tcol[1]], True, True,
                      deps=tokg + C.tok_consts + prev_b47)
        mm_s2b = e_mm(C, bank(C, 4)[:, 256:512], C.c_e16x2[32:40, :],
                      selA[32:40, tcol[0]:tcol[1]], True, True, deps=tokg)
        mm_s1a = e_mm(C, bank(C, 5)[:, 0:256], C.c_e16x1[0:8, :],
                      selA[0:8, tcol[0]:tcol[1]], True, True,
                      deps=tokg + prev_b47)
        mm_s1b = e_mm(C, bank(C, 5)[:, 256:512], C.c_e16x1[32:40, :],
                      selA[32:40, tcol[0]:tcol[1]], True, True, deps=tokg)
        mm_w1 = e_mm(C, bank(C, 6)[:, 0:256], C.c_ones[0:1, :],
                     wB[0:1, tcol[0]:tcol[1]], True, True,
                     deps=tokg + prev_b47)
        mm_w2 = e_mm(C, bank(C, 6)[:, 256:512], C.c_ones[32:33, :],
                     wB[32:33, tcol[0]:tcol[1]], True, True, deps=tokg)
        tm = []
        for j, (src, off) in enumerate(((r1, 0), (r1, 256), (r3, 0), (r3, 256))):
            tm.append(e_tt(C, rm[:, j * 256:(j + 1) * 256],
                           src[:, tcol[0]:tcol[1]],
                           bank(C, 4)[:, off:off + 256], ALU.mult,
                           deps=[mm_s2b] + tok_r + C._ffn_dbg))
        umm = [None, None]
        for fc in range(FC):
            wg = e_dma(C, "sp", wslot(0, 8).rearrange("p (g c) -> p g c", g=8),
                       C.w["wg1T"][:, fc * 128:(fc + 1) * 128]
                       .rearrange("(g p) c -> p g c", g=8), deps=prev_dma)
            wu = e_dma(C, "sp", wslot(8, 8).rearrange("p (g c) -> p g c", g=8),
                       C.w["wu1T"][:, fc * 128:(fc + 1) * 128]
                       .rearrange("(g p) c -> p g c", g=8), deps=prev_dma)
            wb1 = e_dma(C, "sp", wslot(16),
                        C.w["b1s"][:, fc * 128:(fc + 1) * 128], deps=prev_dma)
            wb3 = e_dma(C, "sp", wslot(17),
                        C.w["b3s"][:, fc * 128:(fc + 1) * 128], deps=prev_dma)
            wa2 = e_dma(C, "sp", wslot(18),
                        C.w["a2s"][fc * 128:(fc + 1) * 128, :], deps=prev_dma)
            prev_dma = []
            m1 = m3 = None
            for g in range(EC):
                m1 = e_mm(C, bank(C, 0)[:, 0:256], wslot(g),
                          xn[g][:, t0c:t0c + 256], g == 0, g == EC - 1,
                          deps=[wg] + tok_xn + prev_b03, skipg=True)
                m3 = e_mm(C, bank(C, 0)[:, 256:512], wslot(8 + g),
                          xn[g][:, t0c:t0c + 256], False, g == EC - 1,
                          deps=[wu], skipg=True)
            d11 = e_mm(C, bank(C, 1)[:, 0:256], wslot(16), rm[:, 0:256],
                       True, True, deps=[wb1, tm[0]] + prev_b03)
            d12 = e_mm(C, bank(C, 1)[:, 256:512], wslot(16), rm[:, 256:512],
                       True, True, deps=[tm[1]])
            d31 = e_mm(C, bank(C, 2)[:, 0:256], wslot(17), rm[:, 512:768],
                       True, True, deps=[wb3, tm[2]] + prev_b03)
            d32 = e_mm(C, bank(C, 2)[:, 256:512], wslot(17), rm[:, 768:1024],
                       True, True, deps=[tm[3]])
            ec1 = e_copy_act(C, C1S, bank(C, 0)[:, 0:256], deps=[m1, m3])
            ec3 = e_copy_act(C, C3S, bank(C, 0)[:, 256:512], deps=[m1, m3])
            prev_b03 = []
            for j, actj in ((0, ACT1), (1, ACT2)):
                dj = bank(C, 1)[:, j * 256:(j + 1) * 256]
                ej = bank(C, 2)[:, j * 256:(j + 1) * 256]
                p1 = e_tt(C, TA, dj, C1S, ALU.add, deps=[ec1, d12])
                p2 = e_act(C, TA, TA, AF.Silu, deps=[p1])
                p3 = e_tt(C, TB, ej, C3S, ALU.add, deps=[ec3, d32])
                p4 = e_tt(C, actj, TA, TB, ALU.mult, deps=[p2, p3])
                prev_b03 += [p1, p3]
                umm[j] = e_mm(C, bank(C, 7)[:, j * 256:(j + 1) * 256],
                              wslot(18), actj, fc == 0 and j == 0,
                              fc == FC - 1,
                              deps=[wa2, p4] + (prev_b47 if fc == 0 else []),
                              skipg=True)
            if b == 0 and half == 0 and tl == 0 and fc == 0 \
                    and "c1s0" in C.dbg:
                dmp = [e_dma(C, "pool", C.dbg["c1s0"], C1S, deps=[ec1]),
                       e_dma(C, "pool", C.dbg["c3s0"], C3S, deps=[ec3]),
                       e_dma(C, "pool", C.dbg["act10"], ACT1,
                             deps=[umm[0]]),
                       e_dma(C, "pool", C.dbg["act20"], ACT2,
                             deps=[umm[1]]),
                       e_dma(C, "pool", C.dbg["rm0"], rm, deps=tm)]
                prev_b03 = prev_b03 + dmp
                C._ffn_dbg += dmp
            q1 = e_tt(C, TA, ACT1, bank(C, 6)[:, 0:256], ALU.mult,
                      deps=[mm_w1] + umm[0:1])
            q2 = e_tt(C, TB, ACT2, bank(C, 6)[:, 256:512], ALU.mult,
                      deps=[mm_w2] + umm[1:2])
            q3 = e_tt(C, mixed[:, fc * 256:(fc + 1) * 256], TA, TB, ALU.add,
                      deps=[q1, q2])
            prev_b03 += [q3]
            prev_dma = [m1, m3, d12, d32, umm[0], umm[1]]
        ue1 = e_copy_act(C, TA, bank(C, 7)[:, 0:256],
                         deps=[umm[0], umm[1], q3])
        ue2 = e_copy_act(C, TB, bank(C, 7)[:, 256:512],
                         deps=[umm[0], umm[1], q3])
        v1 = e_tt(C, ACT1, TA, bank(C, 5)[:, 0:256], ALU.mult,
                  deps=[ue1, mm_s1b])
        v1b = e_stt(C, ACT1, ACT1, 2.0, bank(C, 6)[:, 0:256], ALU.mult,
                    ALU.mult, deps=[v1, mm_w1])
        v2 = e_tt(C, ACT2, TB, bank(C, 5)[:, 256:512], ALU.mult,
                  deps=[ue2, mm_s1b])
        v2b = e_stt(C, ACT2, ACT2, 2.0, bank(C, 6)[:, 256:512], ALU.mult,
                    ALU.mult, deps=[v2, mm_w2])
        vs = e_tt(C, TC_, ACT1, ACT2, ALU.add, deps=[v1b, v2b])
        if b == 0 and half == 0 and tl == 0:
            if "mix0" in C.dbg:
                C._ffn_dbg.append(e_dma(C, "pool", C.dbg["mix0"], mixed,
                                        deps=[q3]))
            if "vs0" in C.dbg:
                C._ffn_dbg.append(e_dma(C, "pool", C.dbg["vs0"], TC_,
                                        deps=[vs]))
        prev_b47 = [v1b, v2b, vs] + tm + C._ffn_dbg
        prev_out = []
        wdreg = {0: [], 1: []}
        for dc in range(EC):
            obank = bank(C, dc // 2)[:, (dc % 2) * 256:(dc % 2 + 1) * 256]
            wo = e_dma(C, "sp", wslot(0, 8).rearrange("p (g c) -> p g c", g=8),
                       C.w["wo"][:, dc * 128:(dc + 1) * 128]
                       .rearrange("(g p) c -> p g c", g=8),
                       deps=prev_dma + prev_out + wdreg[1])
            wb2 = e_dma(C, "sp", wslot(19),
                        C.w["b2s"][:, dc * 128:(dc + 1) * 128],
                        deps=prev_dma + prev_out)
            mm = None
            for e in range(EC):
                mm = e_mm(C, obank, wslot(e),
                          ao[e][:, tcol[0]:tcol[1]], e == 0, False,
                          deps=[wo] + (C.tok_ao if e == 0 else [])
                          + prev_b03 + prev_out)
            mmb = e_mm(C, obank, wslot(19), TC_, False, False,
                       deps=[wb2, vs])
            wdreg = {0: [], 1: [mm]}
            for q4 in range(4):
                reg = 8 if q4 % 2 == 0 else 0
                wd = e_dma(C, "sp", wslot(reg, 8)
                           .rearrange("p (g c) -> p g c", g=8),
                           C.w["wd1T"][q4 * 1024:(q4 + 1) * 1024,
                                       dc * 128:(dc + 1) * 128]
                           .rearrange("(g p) c -> p g c", g=8),
                           deps=wdreg[q4 % 2] + [mmb])
                for g in range(EC):
                    fc = q4 * 8 + g
                    mm = e_mm(C, obank, wslot(reg + g),
                              mixed[:, fc * 256:(fc + 1) * 256], False,
                              fc == FC - 1, deps=[wd, q3])
                wdreg[q4 % 2] = [mm]
            xl = e_dma(C, "sp", X1ST, C.x[b][dc * 128:(dc + 1) * 128,
                                             t0c:t0c + 256],
                       deps=prev_out + war0)
            ex = e_stt(C, X1ST, obank, 1.0, X1ST, ALU.mult, ALU.add,
                       deps=[mm, xl])
            so = e_dma(C, "pool", C.x1scr[b][dc * 128:(dc + 1) * 128,
                                             t0c:t0c + 256], X1ST, deps=[ex])
            prev_out = [ex, so]
            prev_dma = [mm]
        prev_b03 = prev_out + [mm]
    C.war_ffn = prev_out + prev_b47 + C._ffn_dbg
    C.wbuf_war = [prev_dma[0]] if prev_dma else prev_out
    ar.pop()


def emit_ln_cols(C, loadf, out_chunks, ncol, affine=None, war=(), nch=EC):
    ar, P = C.ar, C.P
    war = list(war) + getattr(C, "_ln_prev", [])
    ar.push()
    xsq = ar.alloc(ncol)
    srow = ar.alloc(ncol)
    r0 = ar.alloc(ncol)
    vrow = ar.alloc(ncol)
    s0 = ar.alloc(ncol)
    toks = []
    mm_s = mm_q = None
    E_ = 128 * nch
    srcs = []
    for g in range(nch):
        ap, dp = loadf(g)
        srcs.append((ap, dp))
        tq = e_act(C, xsq, ap, AF.Square,
                   deps=dp + list(war) + ([mm_q] if mm_q else []))
        mm_s = e_mm(C, bank(C, 0)[0:1, 0:ncol], C.c_ones[:, 0:1], ap,
                    g == 0, g == nch - 1, deps=dp + list(war) + C.tok_consts)
        mm_q = e_mm(C, bank(C, 1)[0:1, 0:ncol], C.c_ones[:, 0:1], xsq,
                    g == 0, g == nch - 1, deps=[tq])
    t1 = e_ts(C, srow[0:1, :], bank(C, 0)[0:1, 0:ncol], 1.0 / E_, ALU.mult,
              deps=[mm_s])
    t2 = e_ts(C, vrow[0:1, :], bank(C, 1)[0:1, 0:ncol], 1.0 / E_, ALU.mult,
              deps=[mm_q])
    t3 = e_tt(C, s0[0:1, :], srow[0:1, :], srow[0:1, :], ALU.mult, deps=[t1])
    t4 = e_tt(C, vrow[0:1, :], vrow[0:1, :], s0[0:1, :], ALU.subtract,
              deps=[t2, t3])
    t5 = e_ts(C, vrow[0:1, :], vrow[0:1, :], LN_EPS, ALU.add, deps=[t4])
    t6 = e_act(C, s0[0:1, :], vrow[0:1, :], AF.Sqrt, deps=[t5])
    t7 = e_recip(C, r0[0:1, :], s0[0:1, :], deps=[t6])
    t8 = e_tt(C, r0[0:1, :], vrow[0:1, :], r0[0:1, :], ALU.mult, deps=[t7])
    t9 = e_tt(C, s0[0:1, :], s0[0:1, :], r0[0:1, :], ALU.add, deps=[t8])
    t10 = e_ts(C, s0[0:1, :], s0[0:1, :], 0.5, ALU.mult, deps=[t9])
    t11 = e_recip(C, r0[0:1, :], s0[0:1, :], deps=[t10])
    t12 = e_mm(C, bank(C, 2)[:, 0:ncol], C.c_ones[0:1, :], srow[0:1, :],
               True, True, deps=[t1])
    t13 = e_mm(C, bank(C, 3)[:, 0:ncol], C.c_ones[0:1, :], r0[0:1, :],
               True, True, deps=[t11])
    for g in range(nch):
        ap, dp = srcs[g]
        o = out_chunks[g][:, 0:ncol]
        ta = e_tt(C, o, ap, bank(C, 2)[:, 0:ncol], ALU.subtract,
                  deps=dp + [t12])
        tb = e_tt(C, o, o, bank(C, 3)[:, 0:ncol], ALU.mult, deps=[ta, t13])
        if affine is not None:
            tb = e_ts(C, o, o, affine[0][:, g:g + 1], ALU.mult,
                      affine[1][:, g:g + 1], ALU.add, deps=[tb] + C.tok_small)
        toks.append(tb)
    C._ln_prev = [t1, t2, mm_s, mm_q] + toks
    ar.pop()
    return toks


def emit_bitnet_wquant(C):
    ar, P = C.ar, C.P
    nc = C.nc
    C.wq_scr = {}
    C.wq_si = {}
    sc6 = ar.alloc(6)
    ar.push()
    chunk = ar.alloc(512)
    acc = ar.alloc(64)
    col = ar.alloc(8)
    war = C.P.barrier()
    for wi, (nm, (R, Cw)) in enumerate(
            (("wg2T", (1024, 4096)), ("wu2T", (1024, 4096)),
             ("wd2T", (4096, 1024)))):
        scrn = nm + "_q"
        C.wq_scr[nm] = C.nc.dram_tensor(scrn, [R, Cw], F32, kind="Internal").ap()
        nr, ncc = R // 128, Cw // 512
        mma = None
        prev = list(war) + getattr(C, "_wq_prev", [])
        k = 0
        for r in range(nr):
            for cc in range(ncc):
                tl = e_dma(C, "sp", chunk,
                           C.w[nm][r * 128:(r + 1) * 128, cc * 512:(cc + 1) * 512],
                           deps=prev)
                tr_ = e_red(C, acc[:, k:k + 1], chunk, ALU.add, absolute=True,
                            deps=[tl])
                prev = [tr_]
                k += 1
        t1 = e_red(C, acc[:, 63:64], acc[:, 0:k], ALU.add, deps=prev)
        mma = e_mm(C, bank(C, 0)[0:1, 0:1], C.c_ones[:, 0:1], acc[:, 63:64],
                   True, True, deps=[t1])
        t2 = e_ts(C, col[0:1, 0:1], bank(C, 0)[0:1, 0:1], 1.0 / (R * Cw),
                  ALU.mult, deps=[mma])
        t3 = e_ts(C, col[0:1, 0:1], col[0:1, 0:1], 1e-5, ALU.max, deps=[t2])
        t4 = e_recip(C, col[0:1, 1:2], col[0:1, 0:1], deps=[t3])
        t5 = e_recip(C, col[0:1, 2:3], col[0:1, 1:2], deps=[t4])
        m1 = e_mm(C, bank(C, 1)[:, 0:2], C.c_ones[0:1, :], col[0:1, 1:3],
                  True, True, deps=[t5])
        sc2 = sc6[:, wi * 2:wi * 2 + 2]
        tb = e_copy(C, sc2, bank(C, 1)[:, 0:2], deps=[m1])
        C.wq_si[nm] = (sc2, tb)
        y1 = ar.alloc(512)
        fsc = ar.alloc(512)
        prev2 = [t5, tb]
        for r in range(nr):
            for cc in range(ncc):
                tl = e_dma(C, "sp", chunk,
                           C.w[nm][r * 128:(r + 1) * 128, cc * 512:(cc + 1) * 512],
                           deps=prev2)
                q1 = e_ts(C, y1, chunk, sc2[:, 0:1], ALU.mult, 0.5,
                          ALU.add, deps=[tl, tb])
                q2 = e_floor(C, chunk, y1, C.iscr[:, 0:512], fsc, deps=[q1])
                q6 = e_ts(C, chunk, chunk, 1.0, ALU.min, -1.0, ALU.max,
                          deps=[q2])
                st = e_dma(C, "pool", C.wq_scr[nm][r * 128:(r + 1) * 128,
                                                   cc * 512:(cc + 1) * 512],
                           chunk, deps=[q6])
                prev2 = [st]
        C._wq_prev = prev2
    ar.pop()


def emit_l1(C, b):
    ar, P = C.ar, C.P
    if b == 0 and not hasattr(C, "wq_scr"):
        emit_bitnet_wquant(C)
    bz = P.barrier()
    th = e_memset(C, C.halo, 0.0, deps=bz)
    tcv = e_memset(C, C.carry, 0.0, deps=bz)
    C.halo_tok = {cc: th for cc in range(CC)}
    C.carry_tok = {(g, s2): tcv for g in range(CC) for s2 in range(D_STATE)}
    for tc in range(4):
        emit_mamba_tc(C, b, tc)
    for tc in range(4):
        emit_bitnet_tc(C, b, tc)


def emit_mamba_tc(C, b, tc):
    ar, P = C.ar, C.P
    t0 = tc * 512
    war0 = P.barrier()
    ar.push()
    xrawp = ar.alloc(16 * 516)
    zsil = ar.alloc(16 * 512)
    wt = ar.alloc(1024)
    ar.push()
    xn2t = [ar.alloc(512) for _ in range(EC)]
    h = [ar.alloc(512) for _ in range(EC)]

    def xl(g):
        ap = xn2t[g]
        return ap, []

    lds = [e_dma(C, "sp", h[g], C.x1scr[b][g * 128:(g + 1) * 128, t0:t0 + 512],
                 deps=war0) for g in range(EC)]
    t_xn2 = emit_ln_cols(C, lambda g: (h[g], [lds[g]]), xn2t, 512, war=war0)
    t_h = emit_ln_cols(C, lambda g: (xn2t[g], [t_xn2[g]]), h, 512,
                       affine=(C.c_nw, C.c_nb), war=war0)
    prev = []
    for cc in range(CC):
        for part, off in ((0, 0), (1, 2048)):
            wl = e_dma(C, "sp", wt[:, 0:1024].rearrange("p (g c) -> p g c", g=8),
                       C.w["win"][:, off + cc * 128:off + (cc + 1) * 128]
                       .rearrange("(g p) c -> p g c", g=8), deps=prev)
            mm = None
            for g in range(EC):
                mm = e_mm(C, bank(C, 4 + part), wt[:, g * 128:(g + 1) * 128],
                          h[g], g == 0, g == EC - 1,
                          deps=[wl, t_h[g]] + prev)
            if part == 0:
                ev = e_copy_act(C, xrawp[:, cc * 516 + 3:cc * 516 + 515],
                                bank(C, 4), deps=[mm])
            else:
                ev = e_act(C, zsil[:, cc * 512:(cc + 1) * 512], bank(C, 5),
                           AF.Silu, deps=[mm])
            prev = [ev, mm]
    tok_win = prev
    C._mdbg = []
    if b == 0 and tc == 0 and "mh0" in C.dbg:
        for g in range(EC):
            C._mdbg.append(e_dma(C, "pool",
                                 C.dbg["mh0"][g * 128:(g + 1) * 128, :],
                                 h[g], deps=t_h + tok_win))
        for cc2 in range(CC):
            C._mdbg.append(e_dma(C, "pool",
                                 C.dbg["mxz0"][cc2 * 128:(cc2 + 1) * 128, :],
                                 xrawp[:, cc2 * 516 + 3:cc2 * 516 + 515],
                                 deps=tok_win))
            C._mdbg.append(e_dma(C, "pool",
                                 C.dbg["mzs0"][cc2 * 128:(cc2 + 1) * 128, :],
                                 zsil[:, cc2 * 512:(cc2 + 1) * 512],
                                 deps=tok_win))
    ar.pop()
    x_ssm = ar.alloc(16 * 512)
    dt = ar.alloc(16 * 512)
    dtx = ar.alloc(16 * 512)
    BC = ar.alloc(512)
    bca = ar.alloc(512)
    bcc = ar.alloc(512)
    dA = ar.alloc(512)
    u_ = ar.alloc(512)
    ring = ar.alloc(2 * 512)
    conv_scr = ar.alloc(512)

    class YView:
        def __getitem__(self, sl):
            cc = sl[1].start // 512
            assert sl[1].stop - sl[1].start == 512
            return xrawp[:, cc * 516:cc * 516 + 512]
    y = YView()

    tok_xs = []
    for cc in range(CC):
        xp = xrawp[:, cc * 516:(cc + 1) * 516]
        hl = C.halo[:, cc * 3:(cc + 1) * 3]
        cwc = C.c_convw[:, cc * 4:(cc + 1) * 4]
        hcp = e_copy(C, xp[:, 0:3], hl, deps=[C.halo_tok[cc]] + tok_win)
        c1 = e_ts(C, conv_scr, xp[:, 0:512], cwc[:, 0:1], ALU.mult,
                  deps=[hcp] + tok_win + tok_xs[-1:] + C._mdbg)
        c2 = e_stt(C, conv_scr, xp[:, 1:513], cwc[:, 1:2], conv_scr,
                   ALU.mult, ALU.add, deps=[c1])
        c3 = e_stt(C, conv_scr, xp[:, 2:514], cwc[:, 2:3], conv_scr,
                   ALU.mult, ALU.add, deps=[c2])
        c4 = e_stt(C, conv_scr, xp[:, 3:515], cwc[:, 3:4], conv_scr,
                   ALU.mult, ALU.add, deps=[c3])
        hcs = e_copy(C, hl, xp[:, 512:515], deps=[c4])
        C.halo_tok[cc] = hcs
        s1 = e_act(C, x_ssm[:, cc * 512:(cc + 1) * 512], conv_scr, AF.Silu,
                   bias=C.c_convb[:, cc:cc + 1],
                   deps=[c4] + C.tok_small + C._mdbg)
        tok_xs.append(s1)
    prev = list(tok_win)
    tok_dt = []
    for co in range(CC):
        mm = None
        for half2 in range(2):
            wl = e_dma(C, "sp", wt[:, 0:1024].rearrange("p (g c) -> p g c", g=8),
                       C.w["dtw"][half2 * 1024:(half2 + 1) * 1024,
                                  co * 128:(co + 1) * 128]
                       .rearrange("(g p) c -> p g c", g=8), deps=prev)
            for g in range(8):
                ci = half2 * 8 + g
                mm = e_mm(C, bank(C, 6), wt[:, g * 128:(g + 1) * 128],
                          x_ssm[:, ci * 512:(ci + 1) * 512], ci == 0,
                          ci == CC - 1, deps=[wl, tok_xs[ci]] + prev)
            prev = [mm]
        dsl = dt[:, co * 512:(co + 1) * 512]
        s1_ = e_ts(C, dsl, bank(C, 6), C.c_dtb[:, co:co + 1], ALU.add,
                   deps=[mm] + C.tok_small)
        s2_ = e_act(C, conv_scr, dsl, AF.Abs, deps=[s1_])
        s3_ = e_act(C, conv_scr, conv_scr, AF.Exp, scale=-1.0, deps=[s2_])
        s4_ = e_act(C, conv_scr, conv_scr, AF.Ln, bias=C.c_one1,
                    deps=[s3_] + C.tok_consts)
        s5_ = e_ts(C, dsl, dsl, 0.0, ALU.max, deps=[s1_, s2_])
        ev = e_tt(C, dsl, dsl, conv_scr, ALU.add, deps=[s4_, s5_])
        prev = [ev, s5_]
        tok_dt.append(ev)
    tok_dtx = [e_tt(C, dtx[:, cc * 512:(cc + 1) * 512],
                    dt[:, cc * 512:(cc + 1) * 512],
                    x_ssm[:, cc * 512:(cc + 1) * 512], ALU.mult,
                    deps=[tok_dt[cc], tok_xs[cc]]) for cc in range(CC)]
    wb = e_dma(C, "sp", wt[:, 0:1024].rearrange("p (g c) -> p g c", g=16),
               C.w["bp"].rearrange("(g p) c -> p g c", g=16), deps=prev)
    wc = e_dma(C, "sp", C.wbuf[:, 0:1024].rearrange("p (g c) -> p g c", g=16),
               C.w["cp"].rearrange("(g p) c -> p g c", g=16),
               deps=getattr(C, "wbuf_war", []))
    mb = mc = None
    for ci in range(CC):
        mb = e_mm(C, bank(C, 7)[0:64, 0:512], wt[:, ci * 64:(ci + 1) * 64],
                  x_ssm[:, ci * 512:(ci + 1) * 512], ci == 0, ci == CC - 1,
                  deps=[wb, tok_xs[ci]])
        mc = e_mm(C, bank(C, 7)[0:64, 512:1024] if False else
                  bank(C, 6)[0:64, 0:512], C.wbuf[:, ci * 64:(ci + 1) * 64],
                  x_ssm[:, ci * 512:(ci + 1) * 512], ci == 0, ci == CC - 1,
                  deps=[wc, tok_xs[ci], prev[0]])
    eb = e_copy_act(C, BC[0:64, :], bank(C, 7)[0:64, 0:512], deps=[mb])
    ec = e_copy_act(C, BC[64:128, :], bank(C, 6)[0:64, 0:512], deps=[mc])
    C.wbuf_war = [mc]
    sb = e_dma(C, "pool", C.bcscr[0:128, :], BC, deps=[eb, ec])
    if b == 0 and tc == 0 and "mxs0" in C.dbg:
        for cc2 in range(CC):
            C._mdbg.append(e_dma(C, "pool",
                                 C.dbg["mxs0"][cc2 * 128:(cc2 + 1) * 128, :],
                                 x_ssm[:, cc2 * 512:(cc2 + 1) * 512],
                                 deps=[tok_xs[cc2]]))
            C._mdbg.append(e_dma(C, "pool",
                                 C.dbg["mdt0"][cc2 * 128:(cc2 + 1) * 128, :],
                                 dt[:, cc2 * 512:(cc2 + 1) * 512],
                                 deps=[tok_dt[cc2]]))
        C._mdbg.append(e_dma(C, "pool", C.dbg["mbc0"], BC, deps=[eb, ec]))
    tyz = []
    for cc in range(CC):
        tyz.append(e_memset(C, y[:, cc * 512:(cc + 1) * 512], 0.0,
                            deps=[tok_xs[cc], C.halo_tok[cc]] + C._mdbg))
    tok_y = emit_scan(C, b, tc, dt, dtx, y, bca, bcc, dA, u_, ring,
                      [sb] + tyz + tok_dtx)
    tok_y3 = []
    for cc in range(CC):
        w1 = e_stt(C, y[:, cc * 512:(cc + 1) * 512],
                   x_ssm[:, cc * 512:(cc + 1) * 512], C.c_dvec[:, cc:cc + 1],
                   y[:, cc * 512:(cc + 1) * 512], ALU.mult, ALU.add,
                   deps=tok_y + [tok_xs[cc]] + C.tok_small)
        w2 = e_tt(C, y[:, cc * 512:(cc + 1) * 512],
                  y[:, cc * 512:(cc + 1) * 512],
                  zsil[:, cc * 512:(cc + 1) * 512], ALU.mult,
                  deps=[w1] + tok_win)
        tok_y3.append(w2)
    if b == 0 and tc == 0 and "my30" in C.dbg:
        for cc2 in range(CC):
            C._mdbg.append(e_dma(C, "pool",
                                 C.dbg["my30"][cc2 * 128:(cc2 + 1) * 128, :],
                                 y[:, cc2 * 512:(cc2 + 1) * 512],
                                 deps=[tok_y3[cc2]]))
    prev = list(tok_y3) + C._mdbg
    st = conv_scr
    for dc in range(EC):
        wl = e_dma(C, "sp", wt[:, 0:1024].rearrange("p (g c) -> p g c", g=8),
                   C.w["wout"][0:1024, dc * 128:(dc + 1) * 128]
                   .rearrange("(g p) c -> p g c", g=8), deps=prev)
        wl2 = e_dma(C, "sp",
                    C.wbuf[:, 0:1024].rearrange("p (g c) -> p g c", g=8),
                    C.w["wout"][1024:2048, dc * 128:(dc + 1) * 128]
                    .rearrange("(g p) c -> p g c", g=8),
                    deps=(C.wbuf_war if dc == 0 else prev))
        mm = None
        for ci in range(CC):
            src = wt[:, (ci % 8) * 128:(ci % 8 + 1) * 128] if ci < 8 else \
                C.wbuf[:, (ci - 8) * 128:(ci - 7) * 128]
            mm = e_mm(C, bank(C, dc % 4)[:, 0:512], src,
                      y[:, ci * 512:(ci + 1) * 512], ci == 0, ci == CC - 1,
                      deps=[wl if ci < 8 else wl2, tok_y3[ci]] + prev)
        xld = e_dma(C, "sp", st, C.x1scr[b][dc * 128:(dc + 1) * 128,
                                            t0:t0 + 512], deps=prev + [mm]
                    if False else prev)
        ex = e_stt(C, st, bank(C, dc % 4)[:, 0:512], 1.0, st, ALU.mult,
                   ALU.add, deps=[mm, xld])
        so = e_dma(C, "pool", C.out[b][dc * 128:(dc + 1) * 128, t0:t0 + 512],
                   st, deps=[ex])
        prev = [ex, so, mm]
        C.wbuf_war = [mm]
    ar.pop()


def emit_scan(C, b, tc, dt, dtx, y, bca, bcc, dA, u_, ring, war):
    ar, P = C.ar, C.P
    toks = []
    prev_b = list(war)
    prev_dA = list(war)
    prev_u = list(war)
    prev_pr = list(war)
    yacc = {g: [t for t in war] for g in range(CC)}
    for s in range(D_STATE):
        lb = e_dma(C, "sp", bca, C.bcscr[s:s + 1, :].to_broadcast([128, 512]),
                   deps=prev_b)
        lc = e_dma(C, "sp", bcc, C.bcscr[64 + s:65 + s, :]
                   .to_broadcast([128, 512]), deps=prev_b)
        prev_b = []
        for g in range(CC):
            ridx = (s * CC + g) % 2
            st_t = ring[:, ridx * 512:(ridx + 1) * 512]
            a1 = e_act(C, dA, dt[:, g * 512:(g + 1) * 512], AF.Exp,
                       scale=C.c_aneg[:, g * 64 + s:g * 64 + s + 1],
                       deps=prev_dA + C.tok_aneg)
            u1 = e_tt(C, u_, dtx[:, g * 512:(g + 1) * 512], bca, ALU.mult,
                      deps=[lb] + prev_u)
            sc = P.comp("dve", lambda e, o=st_t, d0=dA, d1=u_,
                        ini=C.carry[:, g * 64 + s:g * 64 + s + 1]:
                        e.tensor_tensor_scan(out=o, data0=d0, data1=d1,
                                             initial=ini, op0=ALU.mult,
                                             op1=ALU.add),
                        deps=[a1, u1, C.carry_tok[(g, s)]])
            prev_dA = [sc]
            prev_u = [sc]
            cs = e_copy(C, C.carry[:, g * 64 + s:g * 64 + s + 1],
                        st_t[:, 511:512], deps=[sc])
            C.carry_tok[(g, s)] = cs
            p1 = e_tt(C, u_ if False else dA, st_t, bcc, ALU.mult,
                      deps=[sc, lc] + prev_pr)
            p2 = e_tt(C, y[:, g * 512:(g + 1) * 512],
                      y[:, g * 512:(g + 1) * 512], dA, ALU.add,
                      deps=[p1] + yacc[g])
            prev_dA = [p2]
            prev_pr = [p2]
            yacc[g] = [p2]
            toks = [p2]
        prev_b = [p1, u1]
    return toks


def emit_bitnet_tc(C, b, tc):
    ar, P = C.ar, C.P
    t0 = tc * 512
    war0 = P.barrier()
    ar.push()
    xn2t = [ar.alloc(512) for _ in range(EC)]
    m = [ar.alloc(512) for _ in range(EC)]
    gu = ar.alloc(FC * 512)
    absx = ar.alloc(512)
    absM = ar.alloc(512)
    rowz = ar.alloc(16)
    wt = ar.alloc(1024)
    scr = ar.alloc(512)
    scr2 = ar.alloc(512)
    lds = [e_dma(C, "sp", m[g], C.x1scr[b][g * 128:(g + 1) * 128, t0:t0 + 512],
                 deps=war0) for g in range(EC)]
    t_xn2 = emit_ln_cols(C, lambda g: (m[g], [lds[g]]), xn2t, 512, war=war0)
    pa = []
    for g in range(EC):
        a1 = e_act(C, absx, xn2t[g], AF.Abs, deps=[t_xn2[g]] + pa)
        a2 = e_red(C, absM[g:g + 1, :] if False else absM[0:1, :],
                   absx, ALU.max, axis="C", eng="pool",
                   deps=[a1] + ([am] if g > 0 else []))
        a3 = e_copy(C, absM[g + 1 if False else 1, :]
                    if False else absM[0:1, :], absM[0:1, :], deps=[a2]) \
            if False else a2
        pa = [a2]
        if g == 0:
            am = e_copy(C, scr[0:1, :], absM[0:1, :], deps=[a2])
        else:
            am = e_tt(C, scr[0:1, :], scr[0:1, :], absM[0:1, :], ALU.max,
                      deps=[a2, am])
    c1 = e_ts(C, scr[0:1, :], scr[0:1, :], 1e-5, ALU.max, deps=[am])
    c2 = e_recip(C, scr2[0:1, :], scr[0:1, :], deps=[c1])
    c3 = e_ts(C, scr2[0:1, :], scr2[0:1, :], 127.0, ALU.mult, deps=[c2])
    c4 = e_recip(C, scr2[32:33, :], scr2[0:1, :], deps=[c3])
    ba = e_mm(C, bank(C, 6)[:, 0:512], C.c_ones[0:1, :], scr2[0:1, :],
              True, True, deps=[c3] + war0)
    bi = e_mm(C, bank(C, 7)[:, 0:512], C.c_ones[32:33, :],
              scr2[32:33, :], True, True, deps=[c4] + war0)
    binv = e_copy_act(C, scr2, bank(C, 7)[:, 0:512], deps=[bi])
    tok_m = []
    for g in range(EC):
        q1 = e_tt(C, absx, xn2t[g], bank(C, 6)[:, 0:512], ALU.mult,
                  deps=[ba, t_xn2[g]])
        q2 = e_ts(C, absx, absx, 0.5, ALU.add, deps=[q1])
        q3 = e_floor(C, m[g], absx, C.iscr[:, 0:512], absM, deps=[q2, lds[g]])
        tok_m.append(q3)
    C._bdbg = []
    if b == 0 and tc == 0 and "bxn0" in C.dbg:
        for g in range(EC):
            C._bdbg.append(e_dma(C, "pool",
                                 C.dbg["bxn0"][g * 128:(g + 1) * 128, :],
                                 xn2t[g], deps=[t_xn2[g]]))
            C._bdbg.append(e_dma(C, "pool",
                                 C.dbg["bm0"][g * 128:(g + 1) * 128, :],
                                 m[g], deps=[tok_m[g]]))
        C._bdbg.append(e_dma(C, "pool", C.dbg["bas0"], scr2, deps=[binv]))
    prev = [] + C._bdbg
    si_g = C.wq_si["wg2T"]
    si_u = C.wq_si["wu2T"]
    si_d = C.wq_si["wd2T"]
    for fc in range(FC):
        wlg = e_dma(C, "sp", wt[:, 0:1024].rearrange("p (g c) -> p g c", g=8),
                    C.wq_scr["wg2T"][:, fc * 128:(fc + 1) * 128]
                    .rearrange("(g p) c -> p g c", g=8), deps=prev)
        mm = None
        for g in range(EC):
            mm = e_mm(C, bank(C, 0), wt[:, g * 128:(g + 1) * 128], m[g],
                      g == 0, g == EC - 1, deps=[wlg, tok_m[g]] + prev)
        e1 = e_copy_act(C, scr, bank(C, 0), deps=[mm])
        g1 = e_stt(C, scr, scr, si_g[0][:, 1:2], scr2, ALU.mult, ALU.mult,
                   deps=[e1, binv, si_g[1]])
        g2 = e_act(C, scr, scr, AF.Silu, deps=[g1])
        wlu = e_dma(C, "sp", wt[:, 0:1024].rearrange("p (g c) -> p g c", g=8),
                    C.wq_scr["wu2T"][:, fc * 128:(fc + 1) * 128]
                    .rearrange("(g p) c -> p g c", g=8), deps=[mm])
        mm2 = None
        for g in range(EC):
            mm2 = e_mm(C, bank(C, 1), wt[:, g * 128:(g + 1) * 128], m[g],
                       g == 0, g == EC - 1, deps=[wlu, tok_m[g]] + prev)
        e2 = e_copy_act(C, gu[:, fc * 512:(fc + 1) * 512], bank(C, 1),
                        deps=[mm2])
        g3 = e_stt(C, gu[:, fc * 512:(fc + 1) * 512],
                   gu[:, fc * 512:(fc + 1) * 512], si_u[0][:, 1:2], scr2,
                   ALU.mult, ALU.mult, deps=[e2, binv, si_u[1]])
        g4 = e_tt(C, gu[:, fc * 512:(fc + 1) * 512], gu[:, fc * 512:(fc + 1) * 512],
                  scr, ALU.mult, deps=[g3, g2])
        prev = [g4, mm2]
    tok_gu = prev
    if b == 0 and tc == 0 and "bgu0" in C.dbg:
        C._bdbg.append(e_dma(C, "pool", C.dbg["bgu0"], gu, deps=tok_gu))
        tok_gu = tok_gu + C._bdbg[-1:]
    pa = []
    am = None
    for fc in range(FC):
        a1 = e_act(C, absx, gu[:, fc * 512:(fc + 1) * 512], AF.Abs,
                   deps=tok_gu + pa)
        a2 = e_red(C, absM[0:1, :], absx, ALU.max, axis="C", eng="pool",
                   deps=[a1] + ([am] if fc > 0 else []))
        pa = [a2]
        if fc == 0:
            am = e_copy(C, scr[0:1, :], absM[0:1, :], deps=[a2])
        else:
            am = e_tt(C, scr[0:1, :], scr[0:1, :], absM[0:1, :], ALU.max,
                      deps=[a2, am])
    c1 = e_ts(C, scr[0:1, :], scr[0:1, :], 1e-5, ALU.max, deps=[am])
    c2 = e_recip(C, scr2[0:1, :], scr[0:1, :], deps=[c1])
    c3 = e_ts(C, scr2[0:1, :], scr2[0:1, :], 127.0, ALU.mult, deps=[c2])
    c4 = e_recip(C, scr2[32:33, :], scr2[0:1, :], deps=[c3])
    ba = e_mm(C, bank(C, 6)[:, 0:512], C.c_ones[0:1, :], scr2[0:1, :],
              True, True, deps=[c3, binv] + tok_m)
    bi = e_mm(C, bank(C, 7)[:, 0:512], C.c_ones[32:33, :], scr2[32:33, :],
              True, True, deps=[c4, binv])
    binv2 = e_copy_act(C, scr2, bank(C, 7)[:, 0:512], deps=[bi])
    tok_mg = []
    for fc in range(FC):
        q1 = e_tt(C, absx, gu[:, fc * 512:(fc + 1) * 512],
                  bank(C, 6)[:, 0:512], ALU.mult, deps=[ba] + tok_gu)
        q2 = e_ts(C, absx, absx, 0.5, ALU.add, deps=[q1])
        q3 = e_floor(C, gu[:, fc * 512:(fc + 1) * 512], absx,
                     C.iscr[:, 0:512], absM, deps=[q2] + pa)
        tok_mg.append(q3)
    idl = C.tok_consts
    prev = []
    for dc in range(EC):
        obank = bank(C, 4 + dc % 2)[:, 0:512]
        mm = None
        for q4 in range(4):
            wld = e_dma(C, "sp", wt[:, 0:1024]
                        .rearrange("p (g c) -> p g c", g=8),
                        C.wq_scr["wd2T"][q4 * 1024:(q4 + 1) * 1024,
                                         dc * 128:(dc + 1) * 128]
                        .rearrange("(g p) c -> p g c", g=8),
                        deps=prev + ([mm] if mm is not None else [])
                        + ([ba, bi] if dc == 0 and q4 == 0 else []))
            for g in range(EC):
                fc = q4 * 8 + g
                mm = e_mm(C, obank, wt[:, g * 128:(g + 1) * 128],
                          gu[:, fc * 512:(fc + 1) * 512], fc == 0,
                          fc == FC - 1, deps=[wld, tok_mg[fc]]
                          + (prev if fc == 0 else []))
        mi = e_mm(C, bank(C, dc % 2)[:, 0:512], C.c_ident, xn2t[dc],
                  True, True, deps=idl + [t_xn2[dc]] + prev)
        ol = e_dma(C, "sp", absx, C.out[b][dc * 128:(dc + 1) * 128,
                                           t0:t0 + 512],
                   deps=prev + war0 + tok_mg[-1:])
        x1_ = e_stt(C, absx, bank(C, dc % 2)[:, 0:512], 1.0, absx, ALU.mult,
                    ALU.add, deps=[mi, ol])
        x2_ = e_stt(C, scr, obank, si_d[0][:, 1:2], scr2, ALU.mult, ALU.mult,
                    deps=[mm, binv2, si_d[1]])
        x3_ = e_tt(C, absx, absx, scr, ALU.add, deps=[x1_, x2_])
        so = e_dma(C, "pool", C.out[b][dc * 128:(dc + 1) * 128, t0:t0 + 512],
                   absx, deps=[x3_])
        prev = [so, x3_, mm, mi]
    ar.pop()


def emit_node_prep(C, b):
    ar, P = C.ar, C.P
    xn, tok_xn = C.s_xn, C.tok_xn
    pack0 = ar.alloc(S)
    pack1 = ar.alloc(S)
    C.s_pack0, C.s_pack1 = pack0, pack1
    ar.push()
    lgfm = ar.alloc(S)
    tz = e_memset(C, lgfm, 0.0, deps=C.P.barrier())
    wr = ar.alloc(8 * 64)
    twr = [e_dma(C, "sp", wr[:, g * 64:(g + 1) * 64],
                 C.w["wr"][g * 128:(g + 1) * 128, :]) for g in range(EC)]
    ev = []
    for tt in range(4):
        bk = bank(C, tt % 2)
        mm = None
        for g in range(EC):
            mm = e_mm(C, bk[0:64, :], wr[:, g * 64:(g + 1) * 64],
                      xn[g][:, tt * 512:(tt + 1) * 512], g == 0, g == EC - 1,
                      deps=[twr[g]] + tok_xn + (ev[-2:-1] if tt >= 2 else []))
        ev.append(e_copy_act(C, lgfm[0:64, tt * 512:(tt + 1) * 512],
                             bk[0:64, :], deps=[mm, tz]))
        if "lg0" in C.dbg and b == 0:
            e_dma(C, "pool", C.dbg["lg0"][0:64, tt * 512:(tt + 1) * 512],
                  lgfm[0:64, tt * 512:(tt + 1) * 512], deps=ev[-1:])
    lgT = ar.alloc(64)
    rmax = ar.alloc(16)
    gsum = ar.alloc(16)
    grec = ar.alloc(16)
    ptok = ar.alloc(64)
    pff = ar.alloc(64)
    frac = ar.alloc(64)
    pfr = ar.alloc(64)
    carry = ar.alloc(64)
    tcar = e_memset(C, carry[0:1, :], 0.0)
    prevs = {}
    tok_np = []
    for tc in range(NODE_TCS):
        c0 = tc * 128
        ttr = e_trans(C, bank(C, 2)[0:128, 0:128], lgfm[0:128, c0:c0 + 128],
                      deps=[ev[tc // 4], tz] + C.tok_consts + prevs.get("tr", []))
        tlg = e_copy(C, lgT, bank(C, 2)[0:128, 0:64], deps=[ttr])
        prevs["tr"] = [tlg]
        dmp = []
        if "lgT0" in C.dbg and b == 0 and tc == 0:
            dmp = [e_dma(C, "pool", C.dbg["lgT0"], lgT, deps=[tlg])]
        v3 = lgT.rearrange("p (h k) -> p h k", h=16)
        t1 = e_red(C, rmax, v3, ALU.max, deps=[tlg] + dmp)
        t2 = e_tt(C, v3, v3, rmax.unsqueeze(2).to_broadcast([128, 16, 4]),
                  ALU.subtract, deps=[t1] + dmp)
        t3 = e_act(C, lgT, lgT, AF.Exp, deps=[t2])
        t4 = e_red(C, gsum, v3, ALU.add, deps=[t3] + prevs.get("gsdmp", []))
        t5 = e_recip(C, grec, gsum, deps=[t4])
        if "gs0" in C.dbg and b == 0 and tc == 0:
            prevs["gsdmp"] = [
                e_dma(C, "pool", C.dbg["gs0"][:, 0:16], gsum, deps=[t4]),
                e_dma(C, "pool", C.dbg["gs0"][:, 16:32], grec, deps=[t5])]
        t6 = e_tt(C, ptok.rearrange("p (h k) -> p h k", h=16), v3,
                  grec.unsqueeze(2).to_broadcast([128, 16, 4]), ALU.mult,
                  deps=[t5, t3] + prevs.get("ptokdmp", []))
        if "ptok0" in C.dbg and b == 0 and tc == 0:
            prevs["ptokdmp"] = [e_dma(C, "pool", C.dbg["ptok0"], ptok,
                                      deps=[t6])]
        if NODE_PHASE < 2:
            tok_np += [t6]
            continue
        m1 = e_mm(C, bank(C, 3)[0:128, 0:64], C.c_trilx, ptok, True, False,
                  deps=[t6] + C.tok_consts + prevs.get("pos", []))
        m2 = e_mm(C, bank(C, 3)[0:128, 0:64], C.c_ones[0:1, :], carry[0:1, :],
                  False, True, deps=[tcar])
        m3 = e_mm(C, bank(C, 5)[0:1, 0:64], C.c_ones[:, 0:1], ptok,
                  True, True, deps=[t6, m2] + ([tcar] if tc else []))
        tcar = e_tt(C, carry[0:1, :], carry[0:1, :], bank(C, 5)[0:1, 0:64],
                    ALU.add, deps=[m3, m2])
        if NODE_PHASE < 3:
            ti0 = e_copy(C, lgT, bank(C, 3)[0:128, 0:64], deps=[m1, m2])
            prevs["pos"] = [ti0]
            tok_np += [ti0]
            continue
        ti = e_copy(C, lgT, bank(C, 3)[0:128, 0:64], deps=[m1, m2])
        tf = e_floor(C, pff, lgT, C.iscr[:, 0:64], frac, deps=[ti])
        tg = e_tt(C, frac, lgT, pff, ALU.subtract, deps=[tf])
        prevs["pos"] = [ti]
        th = e_tt(C, pfr, ptok, frac, ALU.mult, deps=[tg, t6])
        if NODE_PHASE < 4:
            tok_np += [th]
            continue
        wrA = prevs.get("fmA", [])
        u1 = e_trans(C, bank(C, 4)[0:64, 0:128], ptok, deps=[th, t6] + wrA)
        u2 = e_trans(C, bank(C, 4)[0:64, 128:256], pff, deps=[tf] + wrA)
        u3 = e_trans(C, bank(C, 4)[0:64, 256:384], pfr, deps=[th] + wrA)
        e1 = e_copy_act(C, pack0[0:64, c0:c0 + 128], bank(C, 4)[0:64, 0:128],
                        deps=[u3])
        e2 = e_copy_act(C, pack0[64:128, c0:c0 + 128],
                        bank(C, 4)[0:64, 128:256], deps=[u3])
        e3 = e_copy_act(C, pack1[0:64, c0:c0 + 128],
                        bank(C, 4)[0:64, 256:384], deps=[u3])
        prevs["fmA"] = [e1, e2, e3]
        tok_np += [e1, e2, e3]
    ar.pop()
    C.tok_np = tok_np
    if tok_np:
        C.war_bank45 = tok_np[-3:]
        C.war_ropepb = tok_np[-3:] + C.tok_xn[-1:]
    if "pfm0" in C.dbg and b == 0:
        for tt in range(4):
            e_dma(C, "pool", C.dbg["pfm0"][0:128, tt * 512:(tt + 1) * 512],
                  pack0[:, tt * 512:(tt + 1) * 512], deps=tok_np)
    if "pfr0" in C.dbg and b == 0:
        for tt in range(4):
            e_dma(C, "pool", C.dbg["pfr0"][0:64, tt * 512:(tt + 1) * 512],
                  pack1[0:64, tt * 512:(tt + 1) * 512], deps=tok_np)


def emit_attn_half(C, b, half, nhp=8):
    toks_out = []
    for hp in range(nhp):
        toks_out += emit_attn_hp(C, b, half, hp)
    C.tok_ao = toks_out
    return toks_out


def _proj_hp(C, wname, cols, dst, war, extra_evac=None):
    ar = C.ar
    xn, tok_xn = C.s_xn, C.tok_xn
    wb = C.wbuf
    wt = []
    for g in range(EC):
        wt.append(e_dma(C, "sp", wb[:, g * 128:(g + 1) * 128],
                        C.w[wname][g * 128:(g + 1) * 128, cols[0]:cols[1]],
                        deps=C.wbuf_war))
    evs = []
    prev_ev = list(war)
    for tt in range(4):
        bk = bank(C, 4 + tt % 2)
        mm = None
        for g in range(EC):
            mm = e_mm(C, bk, wb[:, g * 128:(g + 1) * 128],
                      xn[g][:, tt * 512:(tt + 1) * 512], g == 0, g == EC - 1,
                      deps=[wt[g]] + tok_xn + (prev_ev if g == 0 else []))
        ev = e_copy_act(C, dst[:, tt * 512:(tt + 1) * 512], bk, deps=[mm])
        prev_ev = [ev]
        if extra_evac is not None:
            prev_ev = prev_ev + extra_evac(tt, bk, mm)
        evs += prev_ev
    C.wbuf_war = [evs[-1]]
    return evs


def emit_attn_hp(C, b, half, hp):
    ar, P = C.ar, C.P
    xn, tok_xn = C.s_xn, C.tok_xn
    h0 = 2 * hp
    ar.push()
    q2 = ar.alloc(S)
    k2 = ar.alloc(S)
    ar.push()
    swb = ar.alloc(S)
    cw = ar.alloc(S)
    sw = ar.alloc(S)
    swn = sw
    tok_cwsw = emit_rope_tables(C, hp, cw, sw)
    war45 = getattr(C, "war_bank45", []) + getattr(C, "war_ffn", [])

    def swap_evac(dst, extra):
        def f(tt, bk, mm):
            out = []
            for hh in range(2):
                r0 = hh * 64
                sl = (tt * 512, (tt + 1) * 512)
                out.append(e_copy_act(C, dst[r0:r0 + 32, sl[0]:sl[1]],
                                      bk[r0 + 32:r0 + 64, :],
                                      deps=[mm] + extra))
                out.append(e_copy_act(C, dst[r0 + 32:r0 + 64, sl[0]:sl[1]],
                                      bk[r0:r0 + 32, :], deps=[mm] + extra))
            return out
        return f

    evq = _proj_hp(C, "wq", (h0 * 64, h0 * 64 + 128), q2,
                   war45 + getattr(C, "war_qk", []),
                   extra_evac=swap_evac(swb, getattr(C, "war_swb", [])))
    tsn = e_tt(C, swn, sw, C.c_rotsgn.to_broadcast([128, S]), ALU.mult,
               deps=tok_cwsw)
    a1 = e_tt(C, q2, q2, cw, ALU.mult, deps=evq + tok_cwsw)
    a2 = e_tt(C, swb, swb, swn, ALU.mult, deps=evq + [tsn])
    a3 = e_tt(C, q2, q2, swb, ALU.add, deps=[a1, a2])
    a4 = e_ts(C, q2, q2, 0.125, ALU.mult, deps=[a3])
    tok_q = [a4]
    evk = _proj_hp(C, "wk", (h0 * 64, h0 * 64 + 128), k2, [evq[-1]],
                   extra_evac=swap_evac(swb, [a2, a3]))
    b1 = e_tt(C, k2, k2, cw, ALU.mult, deps=evk + tok_cwsw)
    b2 = e_tt(C, swb, swb, swn, ALU.mult, deps=evk)
    b3 = e_tt(C, k2, k2, swb, ALU.add, deps=[b1, b2])
    tok_k = [b3]
    C.war_bank45 = [evk[-1]]
    C.war_swb = [b2, b3]
    C.war_qk = [a4, b3]
    if "qr0" in C.dbg and b == 0 and hp == 0 and half == 0:
        for tt in range(4):
            e_dma(C, "pool", C.dbg["qr0"][:, tt * 512:(tt + 1) * 512],
                  q2[:, tt * 512:(tt + 1) * 512], deps=tok_q)
            e_dma(C, "pool", C.dbg["kr0"][:, tt * 512:(tt + 1) * 512],
                  k2[:, tt * 512:(tt + 1) * 512], deps=tok_k)
    ar.pop()
    r = emit_attn_core(C, b, half, hp, q2, k2, tok_q, tok_k)
    ar.pop()
    return r


def emit_attn_core(C, b, half, hp, q2, k2, tok_q, tok_k):
    ar, P = C.ar, C.P
    ao = C.s_ao
    pack0 = C.s_pack0
    h0 = 2 * hp
    ar.push()
    v2 = ar.alloc(S)
    vT = ar.alloc(S)
    stage2 = ar.alloc(S)
    lognw = ar.alloc(S)
    pstg = ar.alloc(512)
    cols = ar.alloc(8)
    war0 = getattr(C, "war_attncore", [])
    evv = _proj_hp(C, "wv", (h0 * 64, h0 * 64 + 128), v2,
                   getattr(C, "war_bank45", []) + getattr(C, "war_swb", [])
                   + getattr(C, "war_qk", []) + war0)
    tvT = []
    pv_war = [evv[-1]]
    for kb in range(16):
        bk = bank(C, 4)[:, (kb % 4) * 128:(kb % 4 + 1) * 128]
        t1 = e_trans(C, bk, v2[:, kb * 128:(kb + 1) * 128],
                     deps=evv if kb == 0 else [pv_war[-1]])
        t2 = e_copy_act(C, vT[:, kb * 128:(kb + 1) * 128], bk, deps=[t1])
        tvT.append(t2)
        pv_war = [t2]
    sg = []
    sgdep = (C.tok_np + war0 + getattr(C, "war_swb", [])
             + getattr(C, "war_qk", []))
    sg.append(e_dma(C, "pool", stage2[0:4, :], pack0[h0 * 4:h0 * 4 + 4, :],
                    deps=sgdep))
    sg.append(e_dma(C, "pool", stage2[32:36, :],
                    pack0[h0 * 4 + 4:h0 * 4 + 8, :], deps=sgdep))
    psA_rdrs = war0
    pv_rdrs = war0
    tr_rdrs = {4: [tvT[-1]], 5: war0}
    prev_tex = []
    prev_tnm = []
    out_toks = []
    for hh in range(2):
        prow = stage2[hh * 32:hh * 32 + 4, :]
        for ql in range(8):
            qt = half * 8 + ql
            L = (qt + 1) * 128
            q0 = qt * 128
            mms = []
            for c4 in range((L + 511) // 512):
                w = min(512, L - c4 * 512)
                mms.append(e_mm(C, C.psA[:, c4 * 512:c4 * 512 + w],
                                prow[:, q0:q0 + 128], prow[:, c4 * 512:c4 * 512 + w],
                                True, True, deps=[sg[hh]] + psA_rdrs))
            tln = []
            for c4 in range((L + 511) // 512):
                w = min(512, L - c4 * 512)
                tln.append(e_act(C, lognw[:, c4 * 512:c4 * 512 + w],
                                 C.psA[:, c4 * 512:c4 * 512 + w], AF.Ln,
                                 bias=C.c_eps8, deps=[mms[c4]] + prev_tnm
                                 + C.tok_consts[-1:]))
            qks = []
            for c4 in range((L + 511) // 512):
                w = min(512, L - c4 * 512)
                qks.append(e_mm(C, C.psA[:, c4 * 512:c4 * 512 + w],
                                q2[hh * 64:(hh + 1) * 64, q0:q0 + 128],
                                k2[hh * 64:(hh + 1) * 64, c4 * 512:c4 * 512 + w],
                                True, True, deps=tok_q + tok_k + [tln[c4]]))
            tsc = []
            for c4 in range((L + 511) // 512):
                w = min(512, L - c4 * 512)
                tsc.append(e_tt(C, lognw[:, c4 * 512:c4 * 512 + w],
                                C.psA[:, c4 * 512:c4 * 512 + w],
                                lognw[:, c4 * 512:c4 * 512 + w], ALU.add,
                                deps=[qks[c4], tln[c4]]))
            psA_rdrs = tsc
            tmk = e_tt(C, lognw[:, L - 128:L], lognw[:, L - 128:L], C.c_cmask,
                       ALU.add, deps=[tsc[-1]] + C.tok_consts)
            trm = e_red(C, cols[:, 0:1], lognw[:, 0:L], ALU.max, negate=True,
                        deps=tsc + [tmk] + prev_tex)
            tex = e_act(C, lognw[:, 0:L], lognw[:, 0:L], AF.Exp,
                        bias=cols[:, 0:1], accum=cols[:, 1:2],
                        deps=[trm] + tsc + [tmk])
            prev_tex = [tex]
            trc = e_recip(C, cols[:, 2:3], cols[:, 1:2], deps=[tex])
            tnm = e_ts(C, lognw[:, 0:L], lognw[:, 0:L], cols[:, 2:3], ALU.mult,
                       deps=[trc, tex])
            prev_tnm = [tnm]
            pvmm = []
            for g4 in range((qt + 4) // 4):
                kbs = list(range(g4 * 4, min(qt + 1, g4 * 4 + 4)))
                bki = 4 + g4 % 2
                t1l = None
                for kb in kbs:
                    t1l = e_trans(C,
                                  bank(C, bki)[:, (kb % 4) * 128:(kb % 4 + 1) * 128],
                                  lognw[:, kb * 128:(kb + 1) * 128],
                                  deps=[tnm] + tr_rdrs[bki])
                n = len(kbs) * 128
                tt2 = e_copy_act(C, pstg[:, 0:n], bank(C, bki)[:, 0:n],
                                 deps=[t1l])
                tr_rdrs[bki] = [tt2]
                for j, kb in enumerate(kbs):
                    pvmm.append(e_mm(C, bank(C, 6)[0:128, 0:128],
                                     vT[:, kb * 128:(kb + 1) * 128],
                                     pstg[:, j * 128:(j + 1) * 128],
                                     kb == 0, kb == qt,
                                     deps=[tt2, tvT[kb]] + pv_rdrs))
                    pv_rdrs = []
            tao = e_copy_act(C, ao[hp][hh * 64:(hh + 1) * 64, ql * 128:(ql + 1) * 128],
                             bank(C, 6)[hh * 64:(hh + 1) * 64, 0:128],
                             deps=[pvmm[-1]])
            pv_rdrs = [tao]
            out_toks.append(tao)
            if "sc0" in C.dbg and b == 0 and half == 1 and hp == 0 and hh == 0 \
                    and ql == 7:
                tdbg = e_dma(C, "pool", C.dbg["sc0"][:, 0:L], lognw[:, 0:L],
                             deps=[tnm])
                prev_tnm = prev_tnm + [tdbg]
    C.war_attncore = [out_toks[-1], psA_rdrs[-1]] + prev_tnm
    ar.pop()
    return out_toks


def emit_rope_tables(C, hp, cw, sw):
    ar, P = C.ar, C.P
    pack0, pack1 = C.s_pack0, C.s_pack1
    tok_np = C.tok_np
    out_toks = []
    ar.push()
    stage = ar.alloc(S)
    repl = ar.alloc(3 * 512)
    scr = ar.alloc(6 * 512)
    war_pb = (getattr(C, "war_ropepb", [])
              + getattr(C, "war_attncore", [])
              + getattr(C, "war_ffn", []))
    for h_local in range(2):
        h = 2 * hp + h_local
        s1 = e_dma(C, "pool", stage[0:4, :], pack0[h * 4:h * 4 + 4, :],
                   deps=tok_np + war_pb)
        s2 = e_dma(C, "pool", stage[32:36, :], pack0[64 + h * 4:64 + h * 4 + 4, :],
                   deps=tok_np + war_pb)
        s3 = e_dma(C, "pool", stage[64:68, :], pack1[h * 4:h * 4 + 4, :],
                   deps=tok_np + war_pb)
        war_pb = []
        for tt in range(4):
            sl = (tt * 512, (tt + 1) * 512)
            mmp = e_mm(C, bank(C, 6), C.c_rep4[0:4, :], stage[0:4, sl[0]:sl[1]],
                       True, True, deps=[s1] + C.tok_consts + war_pb)
            mmf = e_mm(C, bank(C, 7), C.c_rep4[32:36, :], stage[32:36, sl[0]:sl[1]],
                       True, True, deps=[s2] + war_pb)
            e1 = e_copy_act(C, repl[:, 0:512], bank(C, 6), deps=[mmp])
            e2 = e_copy_act(C, repl[:, 512:1024], bank(C, 7), deps=[mmf])
            mmr = e_mm(C, bank(C, 6), C.c_rep4[64:68, :], stage[64:68, sl[0]:sl[1]],
                       True, True, deps=[s3, e1])
            e3 = e_copy_act(C, repl[:, 1024:1536], bank(C, 6), deps=[mmr])
            war_pb = [e3]
            p_kd = repl[:, 0:512]
            pf_kd = repl[:, 512:1024]
            pfr_kd = repl[:, 1024:1536]
            arg = scr[:, 0:512]
            s_t = scr[:, 512:1024]
            c_t = scr[:, 1024:1536]
            t3 = scr[:, 1536:2048]
            t6 = scr[:, 2048:2560]
            tmp = scr[:, 2560:3072]
            g1 = e_ts(C, arg, pf_kd, C.c_f_col, ALU.mult, deps=[e2])
            g2 = e_ts(C, tmp, arg, INV_2PI, ALU.mult, 0.5, ALU.add, deps=[g1])
            g3 = e_floor(C, s_t, tmp, C.iscr[:, 0:512], t3, deps=[g2])
            g4a = e_stt(C, tmp, s_t, -C1_2PI, arg, ALU.mult, ALU.add,
                        deps=[g3, g1])
            g4b = e_stt(C, tmp, s_t, -C2_2PI, tmp, ALU.mult, ALU.add, deps=[g4a])
            g4 = e_stt(C, tmp, s_t, -C3_2PI, tmp, ALU.mult, ALU.add, deps=[g4b])
            g5 = e_act(C, s_t, tmp, AF.Sin, deps=[g4])
            g6 = e_ts(C, tmp, arg, HALF_PI, ALU.add, deps=[g4, g1, g5])
            g7 = e_ts(C, t6, tmp, INV_2PI, ALU.mult, 0.5, ALU.add, deps=[g6])
            g8 = e_floor(C, c_t, t6, C.iscr[:, 512:1024], t3, deps=[g7])
            g9a = e_stt(C, tmp, c_t, -C1_2PI, tmp, ALU.mult, ALU.add, deps=[g8])
            g9b = e_stt(C, tmp, c_t, -C2_2PI, tmp, ALU.mult, ALU.add, deps=[g9a])
            g9 = e_stt(C, tmp, c_t, -C3_2PI, tmp, ALU.mult, ALU.add, deps=[g9b])
            g10 = e_act(C, c_t, tmp, AF.Sin, deps=[g9])
            h1 = e_ts(C, t3, c_t, C.c_cm1_col, ALU.mult, deps=[g10])
            h2 = e_ts(C, tmp, s_t, C.c_sf_col, ALU.mult, deps=[g5, g9, g10])
            h3 = e_tt(C, t3, t3, tmp, ALU.subtract, deps=[h1, h2])
            h4 = e_ts(C, t6, s_t, C.c_cm1_col, ALU.mult, deps=[g5])
            h5 = e_ts(C, tmp, c_t, C.c_sf_col, ALU.mult, deps=[g10, h3])
            h6 = e_tt(C, t6, t6, tmp, ALU.add, deps=[h4, h5])
            w1 = e_tt(C, c_t, p_kd, c_t, ALU.mult, deps=[e1, h5, g10])
            w2 = e_tt(C, t3, pfr_kd, t3, ALU.mult, deps=[e3, h3])
            w3 = e_tt(C, c_t, c_t, t3, ALU.add, deps=[w1, w2])
            w4 = e_tt(C, s_t, p_kd, s_t, ALU.mult, deps=[e1, h4, g5])
            w5 = e_tt(C, t6, pfr_kd, t6, ALU.mult, deps=[e3, h6])
            w6 = e_tt(C, s_t, s_t, t6, ALU.add, deps=[w4, w5])
            mc = e_mm(C, bank(C, 7)[0:64, :], C.c_kdsum, c_t, True, True,
                      deps=[w3, e2])
            ms = e_mm(C, bank(C, 6)[0:64, :], C.c_kdsum, s_t, True, True,
                      deps=[w6, e3])
            r0 = h_local * 64
            oc = e_copy_act(C, cw[r0:r0 + 64, sl[0]:sl[1]],
                            bank(C, 7)[0:64, :], deps=[mc])
            os_ = e_copy_act(C, sw[r0:r0 + 64, sl[0]:sl[1]],
                             bank(C, 6)[0:64, :], deps=[ms])
            war_pb += [oc, os_, w3, w6]
            out_toks += [oc, os_]
    C.war_ropepb = [out_toks[-1]]
    ar.pop()
    return out_toks


def emit_ln_plain(C, xdram, xn_chunks, war=(), dbg=None):
    ar = C.ar
    P = C.P
    war = list(war) + getattr(C, "_ln_prev", [])
    toks = []
    ar.push()
    xc = [ar.alloc(512) for _ in range(2)]
    xsq = ar.alloc(512)
    last_use = {0: list(war), 1: list(war)}
    stat_readers = {0: [], 1: []}
    bcast_readers = {0: [], 1: []}
    for tt in range(4):
        psS = C.psA if tt % 2 == 0 else C.psB
        c0, c1 = tt * 512, (tt + 1) * 512
        mm_s = None
        mm_q = None
        for g in range(EC):
            buf = xc[g % 2]
            tl = e_dma(C, "sp", buf, xdram[g * 128:(g + 1) * 128, c0:c1],
                       deps=last_use[g % 2])
            tq = e_act(C, xsq, buf, AF.Square, deps=[tl] + ([mm_q] if mm_q else []))
            mm_s = e_mm(C, psS[0:1, 0:512], C.c_ones[:, 0:1], buf,
                        g == 0, g == EC - 1,
                        deps=[tl] + C.tok_consts + stat_readers[tt % 2])
            mm_q = e_mm(C, psS[0:1, 512:1024], C.c_ones[:, 0:1], xsq,
                        g == 0, g == EC - 1, deps=[tq] + stat_readers[tt % 2])
            last_use[g % 2] = [mm_s, tq]
        ar.push()
        srow = ar.alloc(512)
        qrow = ar.alloc(512)
        vrow = ar.alloc(512)
        s0 = ar.alloc(512)
        r0 = ar.alloc(512)
        t1 = e_ts(C, srow[0:1, :], psS[0:1, 0:512], 1.0 / E, ALU.mult, deps=[mm_s])
        t2 = e_ts(C, qrow[0:1, :], psS[0:1, 512:1024], 1.0 / E, ALU.mult, deps=[mm_q])
        t3 = e_tt(C, vrow[0:1, :], srow[0:1, :], srow[0:1, :], ALU.mult, deps=[t1])
        t4 = e_tt(C, vrow[0:1, :], qrow[0:1, :], vrow[0:1, :], ALU.subtract,
                  deps=[t2, t3])
        t5 = e_ts(C, vrow[0:1, :], vrow[0:1, :], LN_EPS, ALU.add, deps=[t4])
        t6 = e_act(C, s0[0:1, :], vrow[0:1, :], AF.Sqrt, deps=[t5])
        t7 = e_recip(C, r0[0:1, :], s0[0:1, :], deps=[t6])
        t8 = e_tt(C, r0[0:1, :], vrow[0:1, :], r0[0:1, :], ALU.mult, deps=[t7, t5])
        t9 = e_tt(C, s0[0:1, :], s0[0:1, :], r0[0:1, :], ALU.add, deps=[t8, t6])
        t10 = e_ts(C, s0[0:1, :], s0[0:1, :], 0.5, ALU.mult, deps=[t9])
        t11 = e_recip(C, r0[0:1, :], s0[0:1, :], deps=[t10])
        stat_readers[tt % 2] = [t1, t2]
        if dbg and "lnstat" in C.dbg:
            e_dma(C, "pool", C.dbg["lnstat"][0:1, c0:c1], srow[0:1, :], deps=[t1])
            e_dma(C, "pool", C.dbg["lnstat"][1:2, c0:c1], qrow[0:1, :], deps=[t2])
            e_dma(C, "pool", C.dbg["lnstat"][2:3, c0:c1], r0[0:1, :], deps=[t11])
        mb = psS[:, 1024:1536]
        rb = psS[:, 1536:2048]
        t12 = e_mm(C, mb, C.c_ones[0:1, :], srow[0:1, :], True, True,
                   deps=[t1] + C.tok_consts + bcast_readers[tt % 2])
        t13 = e_mm(C, rb, C.c_ones[0:1, :], r0[0:1, :], True, True,
                   deps=[t11] + bcast_readers[tt % 2])
        rdrs = []
        for g in range(EC):
            buf = xc[g % 2]
            tl = e_dma(C, "sp", buf, xdram[g * 128:(g + 1) * 128, c0:c1],
                       deps=last_use[g % 2])
            o = xn_chunks[g][:, c0:c1]
            ta = e_tt(C, o, buf, mb, ALU.subtract, deps=[tl, t12])
            tb = e_tt(C, o, o, rb, ALU.mult, deps=[ta, t13])
            last_use[g % 2] = [ta]
            rdrs += [ta, tb]
            toks.append(tb)
            if dbg and dbg in C.dbg:
                e_dma(C, "pool", C.dbg[dbg][g * 128:(g + 1) * 128, c0:c1], o,
                      deps=[tb])
        bcast_readers[tt % 2] = rdrs
        ar.pop()
    C._ln_prev = toks[-4:] + stat_readers[0] + stat_readers[1]
    ar.pop()
    return toks



def kernel(**inputs):
    wl = host_layout(inputs)
    cst = host_constants()
    x = np.asarray(inputs["L0_attn_x"], dtype=np.float32)
    xT = np.ascontiguousarray(x.transpose(0, 2, 1))
    C = build_nc()
    build_program(C)
    in_maps = []
    for c in range(N_CORES):
        m = {"x": np.ascontiguousarray(xT[c * B_LOC:(c + 1) * B_LOC])}
        m.update(wl)
        m.update(cst)
        in_maps.append(m)
    res = run_bass_kernel_spmd(C.nc, in_maps, core_ids=list(range(N_CORES)))
    out = np.stack([r["out"] for r in res.results], axis=0)
    out = out.reshape(B_TOT, E, S).transpose(0, 2, 1)
    if DEBUG:
        kernel.dbg = [{k: r["dbg_" + k] for k, _ in DEBUG} for r in res.results]
    return np.ascontiguousarray(out)
